# version 07 (final)
# speedup vs baseline: 1.1480x; 1.0004x over previous
import numpy as np
import ml_dtypes
from contextlib import ExitStack
import concourse.bass as bass
import concourse.mybir as mybir
from concourse.bass_utils import run_bass_kernel_spmd

F32 = mybir.dt.float32
BF16 = mybir.dt.bfloat16
FP8 = mybir.dt.float8e4
ALU = mybir.AluOpType
ACT = mybir.ActivationFunctionType
AX = mybir.AxisListType

T = 8192
D = 1024
HD = 128
OG = 8
W = 512
NCH = T // W
SG = 3
NKT = T // 128
NSG = (NKT + SG - 1) // SG
KPC = W // 128
ISQ = float(1.0 / np.sqrt(128.0))

UPTO = 3
TAPS = ()
_CHECK_SCHED = [None]
_HALT = [None]
_STAGES = ["p0dma", "p0cast", "p0sw", "p0bc", "p0vec", "p0kq", "p0",
           "p1stats", "p1proj", "p1rope", "p1tr", "p1nrm", "p2",
           "p3n", "p3w", "p3ln3", "p3mix", "p3scan", "p3wo", "p3ln4", "full"]


def _H(stage):
    if _HALT[0] is None:
        return True
    return _STAGES.index(stage) <= _STAGES.index(_HALT[0])


class _DummyInstr:
    def then_inc(self, *a, **k):
        return self


class _DummyEng:
    def __init__(self, name="?", sched=None):
        self._name = name
        self._sched = sched

    def __getattr__(self, name):
        if name == "wait_ge":
            return lambda *a, **k: None
        return lambda *a, **k: _DummyInstr()


class Sched:
    def __init__(self):
        self.counts = {}
        self.ev = {}
        self.mode = "count"
        self.sems = {}
        self.trace = None

    def inc(self, instr, sem, key=None, n=1):
        c = self.counts.get(sem, 0) + n
        self.counts[sem] = c
        if self.mode == "count":
            if self.trace is not None and self._cur is not None:
                self.trace[self._cur].append(("inc", sem, n, key, c))
            if key is not None:
                assert key not in self.ev, f"dup event {key}"
                self.ev[key] = (sem, c)
        else:
            if key is not None:
                assert self.ev[key] == (sem, c), (key, self.ev[key], (sem, c))
            instr.then_inc(self.sems[sem], n)
        return instr

    def wait(self, eng, key):
        if self.mode == "count":
            assert isinstance(key, tuple)
            if self.trace is not None and getattr(eng, "_name", None):
                self.trace[eng._name].append(("wait", key))
            return
        sem, val = self.ev[key]
        eng.wait_ge(self.sems[sem], val)

    def sew(self, eng, sem):
        c = self.counts.get(sem, 0)
        if self.mode == "count":
            if self.trace is not None and getattr(eng, "_name", None):
                self.trace[eng._name].append(("wait_val", sem, c))
            return
        eng.wait_ge(self.sems[sem], c)

    def wait_opt(self, eng, key):
        if self.mode == "count":
            if self.trace is not None and getattr(eng, "_name", None):
                self.trace[eng._name].append(("wait_opt", key))
            return
        if key in self.ev:
            sem, val = self.ev[key]
            eng.wait_ge(self.sems[sem], val)

    def se(self, instr, eng, sem):
        c = self.counts.get(sem, 0) + 1
        self.counts[sem] = c
        if self.mode == "count":
            if self.trace is not None and getattr(eng, "_name", None):
                self.trace[eng._name].append(("inc", sem, 1, None, c))
                self.trace[eng._name].append(("wait_val", sem, c))
            return
        instr.then_inc(self.sems[sem], 1)
        eng.wait_ge(self.sems[sem], c)


def build_nc(upto=3, taps=()):
    taps = set(taps)
    nc = bass.Bass("TRN2", target_bir_lowering=False)
    dram = {}

    def din(name, shape, dt):
        dram[name] = nc.dram_tensor(name, list(shape), dt, kind="ExternalInput")

    def dout(name, shape, dt):
        dram[name] = nc.dram_tensor(name, list(shape), dt, kind="ExternalOutput")

    din("xT", (D, T), F32)
    din("wqT", (D, HD), F32)
    din("wkT", (D, HD), F32)
    din("wvT", (D, HD), F32)
    din("woT", (HD, D), F32)
    for n in ("WkT", "WvT", "WrT", "WoT"):
        din(n, (D, D), F32)
    din("fker", (D, D), BF16)
    for n in ("td", "tf", "tmk", "tmv", "tmr"):
        din(n, (128, OG), F32)
    din("fbias", (128, OG), F32)
    din("s_in", (128, 1), F32)
    din("s_k", (128, 1), F32)
    din("cosT", (128, T), BF16)
    din("sinT", (128, T), BF16)
    din("ident", (128, 128), F32)
    din("swp", (128, 128), BF16)
    din("aconsts", (128, 8), F32)
    dout("y", (D, T), F32)
    dram["x2d"] = nc.dram_tensor("x2scratch", [D, T], F32, kind="Internal")
    for tname in taps:
        if tname in ("q_rot", "k_rot", "v_tok"):
            dout("tap_" + tname, (128, T), BF16)
        elif tname == "negC":
            dout("tap_negC", (128, 1), F32)
        elif tname == "x2":
            dout("tap_x2", (D, T), F32)

    S = Sched()
    S._cur = None
    if _CHECK_SCHED[0] is not None:
        S.trace = _CHECK_SCHED[0]

    with ExitStack() as st0:
        def sb0(name, shape, dt):
            return st0.enter_context(nc.sbuf_tensor(name, list(shape), dt))

        for s_ in ("din", "dxa", "dxb", "dcsa", "dcsb", "dwa", "dfk", "dwl",
                   "dx2", "dq", "dtap", "dout", "pe", "dve", "act", "gp",
                   "vsf", "asf", "psf", "gsf"):
            S.sems[s_] = st0.enter_context(nc.semaphore(s_))

        ones_bf = sb0("ones_bf", (128, 1), BF16)
        ones1_bf = sb0("ones1_bf", (1, 128), BF16)
        acst = sb0("acst", (128, 8), F32)
        kqb = sb0("kqb", (128, OG * D), BF16)
        lam_fm = sb0("lam_fm", (128, OG), F32)
        lam_bf = sb0("lam_bf", (128, OG), BF16)
        eu_fm = sb0("eu_fm", (128, OG), F32)
        eu_bf = sb0("eu_bf", (128, OG), BF16)
        tmk_fm = sb0("tmk_fm", (128, OG), F32)
        tmv_fm = sb0("tmv_fm", (128, OG), F32)
        tmr_fm = sb0("tmr_fm", (128, OG), F32)
        omk_fm = sb0("omk_fm", (128, OG), F32)
        omv_fm = sb0("omv_fm", (128, OG), F32)
        omr_fm = sb0("omr_fm", (128, OG), F32)
        bias_fm = sb0("bias_fm", (128, OG), F32)
        sin_sb = sb0("sin_sb", (128, 1), F32)
        sk_sb = sb0("sk_sb", (128, 1), F32)
        rs_in_b = sb0("rs_in_b", (128, 1), F32)
        rs_k_b = sb0("rs_k_b", (128, 1), F32)
        sprod_b = sb0("sprod_b", (128, 1), F32)
        negC_b = sb0("negC_b", (128, 1), F32)
        smalls = sb0("smalls", (1, 16), F32)
        vsm = sb0("vsm", (128, 2 * OG), F32)
        stat_mu = sb0("stat_mu", (1, W), F32)
        stat_m2 = sb0("stat_m2", (1, W), F32)
        stat_is = sb0("stat_is", (1, W), F32)
        stat_mn = sb0("stat_mn", (1, W), BF16)
        stat_isb = sb0("stat_isb", (1, W), BF16)
        lrow = sb0("lrow", (1, W), F32)

        with ExitStack() as stA:
            def sbA(name, shape, dt):
                return stA.enter_context(nc.sbuf_tensor(name, list(shape), dt))

            def psA(name, shape, dt=F32):
                return stA.enter_context(nc.psum_tensor(name, list(shape), dt))

            ident_b = sbA("ident_b", (128, 128), F32)
            swp_b = sbA("swp_b", (128, 128), BF16)
            lrow_bf = sbA("lrow_bf", (1, W), BF16)
            srow = sbA("srow", (1, W), F32)
            stat_bv = sbA("stat_bv", (1, W), BF16)
            wq_b = sbA("wq_b", (128, D), BF16)
            wk_b = sbA("wk_b", (128, D), BF16)
            wv_b = sbA("wv_b", (128, D), BF16)
            wo_b = sbA("wo_b", (128, D), BF16)
            SWq = sbA("SWq", (1, 128), BF16)
            SWk = sbA("SWk", (1, 128), BF16)
            SWv = sbA("SWv", (1, 128), BF16)
            cos_b = sbA("cos_b", (128, 2 * W), BF16)
            sin_b = sbA("sin_b", (128, 2 * W), BF16)
            q_rot = sbA("q_rot", (128, T), BF16)
            k_rot = sbA("k_rot", (128, T), BF16)
            v_tok = sbA("v_tok", (128, T), BF16)
            xcf = sbA("xcf", (128, 2 * OG * W), F32)
            xcb = sbA("xcb", (128, 2 * OG * W), BF16)
            xsq = sbA("xsq", (128, OG * W), BF16)
            qstg = sbA("qstg", (128, W), BF16)
            kstg = sbA("kstg", (128, W), BF16)
            vstg = sbA("vstg", (128, W), F32)
            av_sb = sbA("av_sb", (128, W), F32)
            u2_scr = sbA("u2_scr", (128, 2 * W), BF16)
            xg = sbA("xg", (128, OG * W), F32)
            qsw = sbA("qsw", (128, W), BF16)
            ksw = sbA("ksw", (128, W), BF16)
            sq_scr = sbA("sq_scr", (128, W), BF16)
            ptil = sbA("ptil", (128, 2 * SG * W), BF16)
            O_sb = qstg
            linv128 = vstg
            xhb = sbA("xhb", (128, OG * W), BF16)
            xq8 = sbA("xq8", (128, 4 * W), FP8)
            xqb = sbA("xqb", (128, OG * W), BF16)
            bcast_sb = sbA("bcast_sb", (128, 2 * W), BF16)
            ybf = sbA("ybf", (128, 2 * W), BF16)
            u_scr = sbA("u_scr", (128, 2 * W), F32)

            ps_q = psA("ps_q", (128, 3 * W))
            ps_x = psA("ps_x", (128, 3 * W))
            ps_o = psA("ps_o", (128, W))
            ps_l = psA("ps_l", (128, W))

            def xf(i, og=None):
                b = (i % 2) * OG * W
                if og is None:
                    return xcf[:, b:b + OG * W]
                return xcf[:, b + og * W: b + (og + 1) * W]

            def xb_(i, og=None):
                b = (i % 2) * OG * W
                if og is None:
                    return xcb[:, b:b + OG * W]
                return xcb[:, b + og * W: b + (og + 1) * W]

            wo_slots = [ps_q[:, 0:W], ps_q[:, W:2 * W], ps_q[:, 2 * W:3 * W],
                        ps_x[:, 0:W]]
            ffn_slots = [ps_q[:, 0:W], ps_q[:, W:2 * W], ps_q[:, 2 * W:3 * W],
                         ps_o[:]]
            tr_slots = [ps_x[:, W + j * 128:W + (j + 1) * 128] for j in range(KPC)]

            def spA(sp):
                def dl(dst, src, key=None):
                    S.inc(sp.dma_start(out=dst, in_=src), "din", key, 16)

                dl(lam_fm[:], dram["td"].ap())
                dl(eu_fm[:], dram["tf"].ap())
                dl(tmk_fm[:], dram["tmk"].ap())
                dl(tmv_fm[:], dram["tmv"].ap())
                dl(tmr_fm[:], dram["tmr"].ap())
                dl(bias_fm[:], dram["fbias"].ap())
                dl(sin_sb[:], dram["s_in"].ap())
                dl(sk_sb[:], dram["s_k"].ap())
                dl(ident_b[:], dram["ident"].ap())
                dl(swp_b[:], dram["swp"].ap())
                dl(acst[:], dram["aconsts"].ap(), key=("p0", "small"))
                for wi, wname in enumerate(("wqT", "wkT", "wvT")):
                    dst = bass.AP(xcf, wi * D, [[2 * OG * W, 128], [128, OG], [1, 128]])
                    src = bass.AP(dram[wname], 0, [[HD, 128], [128 * HD, OG], [1, 128]])
                    S.inc(sp.dma_start(out=dst, in_=src), "dwa", None, 16)
                S.inc(sp.dma_start(out=xcf[:, 3 * D:4 * D], in_=dram["woT"].ap()),
                      "dwa", ("p0", "wa"), 16)
                dst = bass.AP(kqb, 0, [[OG * D, 128], [D, OG], [1, D]])
                src = bass.AP(dram["fker"], 0, [[D, 128], [128 * D, OG], [1, D]])
                S.inc(sp.dma_start(out=dst, in_=src), "dfk", ("p0", "fker"), 16)
                if not _H("p1stats"):
                    return
                for i in range(NCH):
                    if i == 0:
                        S.wait(sp, ("p0", "wcast"))
                    if i >= 2:
                        S.wait(sp, ("p1", "cast", i - 2))
                    dst = bass.AP(xcf, (i % 2) * OG * W,
                                  [[2 * OG * W, 128], [W, OG], [1, W]])
                    src = bass.AP(dram["xT"], i * W, [[T, 128], [128 * T, OG], [1, W]])
                    S.inc(sp.dma_start(out=dst, in_=src),
                          "dxa" if i % 2 == 0 else "dxb", ("p1", "x", i), 16)
                    if not _H("p1rope"):
                        continue
                    if i >= 2:
                        S.wait(sp, ("p1", "evac", i - 1))
                    cb = (i % 2) * W
                    dcs = "dcsa" if i % 2 == 0 else "dcsb"
                    S.inc(sp.dma_start(out=cos_b[:, cb:cb + W],
                                       in_=bass.AP(dram["cosT"], i * W,
                                                   [[T, 128], [1, W]])), dcs, None, 16)
                    S.inc(sp.dma_start(out=sin_b[:, cb:cb + W],
                                       in_=bass.AP(dram["sinT"], i * W,
                                                   [[T, 128], [1, W]])),
                          dcs, ("p1", "cs", i), 16)
                if not _H("p2"):
                    return
                if taps & {"q_rot", "k_rot", "v_tok", "negC"}:
                    S.wait(sp, ("p1", "done"))
                    for tname, tsrc in (("q_rot", q_rot), ("k_rot", k_rot),
                                        ("v_tok", v_tok)):
                        if tname in taps:
                            S.inc(sp.dma_start(out=dram["tap_" + tname].ap(),
                                               in_=tsrc[:]), "dtap", ("tap", tname), 16)
                    if "negC" in taps:
                        S.inc(sp.dma_start(out=dram["tap_negC"].ap(), in_=negC_b[:]),
                              "dtap", ("tap", "negC"), 16)
                if upto < 2:
                    S.wait(sp, ("p1", "done"))
                    if S.mode == "emit" and S.counts.get("dtap", 0):
                        sp.wait_ge(S.sems["dtap"], S.counts["dtap"])
                    return
                for i in range(NCH + 2):
                    if i >= 2:
                        j = i - 2
                        S.wait(sp, ("p2", "resid", j))
                        src = bass.AP(xcf, (j % 2) * OG * W,
                                      [[2 * OG * W, 128], [W, OG], [1, W]])
                        dst = bass.AP(dram["x2d"], j * W,
                                      [[T, 128], [128 * T, OG], [1, W]])
                        S.inc(sp.dma_start(out=dst, in_=src), "dx2",
                              ("p2", "x2out", j), 16)
                        if "x2" in taps:
                            dstt = bass.AP(dram["tap_x2"], j * W,
                                           [[T, 128], [128 * T, OG], [1, W]])
                            S.inc(sp.dma_start(out=dstt, in_=src), "dtap",
                                  ("tapx2", j), 16)
                    if i < NCH:
                        if i == 0:
                            S.wait(sp, ("p1", "done"))
                        if i >= 2:
                            S.wait(sp, ("p2", "x2out", i - 2))
                        dst = bass.AP(xcf, (i % 2) * OG * W,
                                      [[2 * OG * W, 128], [W, OG], [1, W]])
                        src = bass.AP(dram["xT"], i * W,
                                      [[T, 128], [128 * T, OG], [1, W]])
                        S.inc(sp.dma_start(out=dst, in_=src),
                              "dxa" if i % 2 == 0 else "dxb", ("p2", "x", i), 16)
                if upto < 3:
                    S.wait(sp, ("p2", "x2out", NCH - 1))
                    if S.mode == "emit" and S.counts.get("dtap", 0):
                        sp.wait_ge(S.sems["dtap"], S.counts["dtap"])

            def peA(pe):
                if not _H("p0sw"):
                    return
                S.wait(pe, ("p0", "wcast"))
                for wi, wsb in enumerate((wq_b, wk_b, wv_b)):
                    for og in range(OG):
                        ins = pe.matmul(out=ps_l[0:1, wi * 128:(wi + 1) * 128],
                                        lhsT=ones_bf[:],
                                        rhs=wsb[:, og * 128:(og + 1) * 128],
                                        start=(og == 0), stop=(og == OG - 1))
                S.inc(ins, "pe", ("p0", "sw"))

                if not _H("p1stats"):
                    return
                for i in range(NCH):
                    S.wait(pe, ("p1", "cast", i))
                    if i >= 1:
                        if _H("p1nrm"):
                            S.wait(pe, ("p1", "nrm", i - 1))
                        elif _H("p1proj"):
                            S.wait(pe, ("p1", "evac", i - 1))
                        else:
                            S.wait(pe, ("p1", "var", i - 1))
                    else:
                        S.wait(pe, ("p0", "bce"))
                    for og in range(OG):
                        pe.matmul(out=ps_l[0:1, :], lhsT=ones_bf[:],
                                  rhs=xb_(i, og), start=(og == 0),
                                  stop=(og == OG - 1), tile_position=(0, 0))
                        i1 = pe.matmul(out=ps_l[32:33, :], lhsT=ones_bf[:],
                                       rhs=xsq[:, og * W:(og + 1) * W],
                                       start=(og == 0), stop=(og == OG - 1),
                                       tile_position=(0, 32))
                    S.inc(i1, "pe", ("p1", "stats", i))
                    if not _H("p1proj"):
                        continue
                    for wi, wsb in enumerate((wq_b, wk_b, wv_b)):
                        for og in range(OG):
                            pe.matmul(out=ps_q[:, wi * W:(wi + 1) * W],
                                      lhsT=wsb[:, og * 128:(og + 1) * 128],
                                      rhs=xb_(i, og), start=(og == 0), stop=False)
                    S.wait(pe, ("p1", "bvec", i))
                    for wi, swv in enumerate((SWq, SWk, SWv)):
                        pe.matmul(out=ps_q[:, wi * W:(wi + 1) * W], lhsT=swv[:],
                                  rhs=stat_bv[:], start=False, stop=True)
                    S.wait(pe, ("p1", "isb", i))
                    S.inc(pe.matmul(out=ps_x[:, 0:W], lhsT=ones1_bf[:],
                                    rhs=stat_isb[:], start=True, stop=True),
                          "pe", ("p1", "proj", i))
                    if not _H("p1rope"):
                        continue
                    S.wait(pe, ("p1", "evac", i))
                    pe.matmul(out=ps_x[:, 2 * W:3 * W], lhsT=swp_b[:], rhs=qstg[:],
                              start=True, stop=True)
                    S.inc(pe.matmul(out=ps_o[:], lhsT=swp_b[:], rhs=kstg[:],
                                    start=True, stop=True),
                          "pe", ("p1", "swmm", i))
                    if not _H("p1tr"):
                        continue
                    if i > 0:
                        S.wait(pe, ("p1", "trevac", i - 1, KPC - 1))
                    for j in range(KPC):
                        S.inc(pe.transpose(out=tr_slots[j],
                                           in_=vstg[:, j * 128:(j + 1) * 128],
                                           identity=ident_b[:]),
                              "pe", ("p1", "tr", i, j))
                    if not _H("p1nrm"):
                        continue
                    S.wait(pe, ("p1", "sq", i))
                    S.inc(pe.matmul(out=ps_o[0:1, :], lhsT=ones_bf[:], rhs=sq_scr[:],
                                    start=True, stop=True), "pe", ("p1", "nq", i))
                    S.wait(pe, ("p1", "sqk", i))
                    S.inc(pe.matmul(out=ps_o[32:33, :], lhsT=ones_bf[:], rhs=sq_scr[:],
                                    start=True, stop=True, tile_position=(0, 32)),
                          "pe", ("p1", "nk", i))
                if not _H("p2"):
                    return
                S.wait(pe, ("p1", "negc_v"))
                S.inc(pe.matmul(out=ps_o[:], lhsT=ones1_bf[:],
                                rhs=lrow_bf[0:1, :], start=True, stop=True),
                      "pe", ("p1", "negc_mm"))
                if upto < 2:
                    return
                for i in range(NCH):
                    if i >= 1:
                        S.wait(pe, ("p2", "fevac", i - 1, OG - 1))
                        S.wait(pe, ("p2", "resid", i - 1))
                    else:
                        S.wait(pe, ("p1", "done"))
                        S.wait(pe, ("p1", "rope", NCH - 1))
                    for g in range(NSG):
                        ks = [g * SG + j for j in range(SG) if g * SG + j < NKT]
                        if g >= 2:
                            gg = g - 2
                            kp = [gg * SG + j for j in range(SG) if gg * SG + j < NKT]
                            S.wait(pe, ("p2", "exp", i, gg))
                            for jj, kt in enumerate(kp):
                                pe.matmul(out=ps_o[:],
                                          lhsT=v_tok[:, kt * 128:(kt + 1) * 128],
                                          rhs=ptil[:, (gg % 2) * SG * W + jj * W:
                                                   (gg % 2) * SG * W + (jj + 1) * W],
                                          start=(gg == 0 and jj == 0),
                                          stop=(kt == NKT - 1))
                                ii = pe.matmul(out=ps_l[0:1, :], lhsT=ones_bf[:],
                                               rhs=ptil[:, (gg % 2) * SG * W + jj * W:
                                                        (gg % 2) * SG * W + (jj + 1) * W],
                                               start=(gg == 0 and jj == 0),
                                               stop=(kt == NKT - 1))
                            S.inc(ii, "pe", ("p2", "pv", i, gg))
                        sps = ps_q if g % 2 == 0 else ps_x
                        for jj, kt in enumerate(ks):
                            ii = pe.matmul(out=sps[:, jj * W:(jj + 1) * W],
                                           lhsT=k_rot[:, kt * 128:(kt + 1) * 128],
                                           rhs=q_rot[:, i * W:(i + 1) * W],
                                           start=True, stop=True)
                        S.inc(ii, "pe", ("p2", "s", i, g))
                    for gg in (NSG - 2, NSG - 1):
                        kp = [gg * SG + j for j in range(SG) if gg * SG + j < NKT]
                        S.wait(pe, ("p2", "exp", i, gg))
                        for jj, kt in enumerate(kp):
                            pe.matmul(out=ps_o[:],
                                      lhsT=v_tok[:, kt * 128:(kt + 1) * 128],
                                      rhs=ptil[:, (gg % 2) * SG * W + jj * W:
                                               (gg % 2) * SG * W + (jj + 1) * W],
                                      start=False, stop=(kt == NKT - 1))
                            ii = pe.matmul(out=ps_l[0:1, :], lhsT=ones_bf[:],
                                           rhs=ptil[:, (gg % 2) * SG * W + jj * W:
                                                    (gg % 2) * SG * W + (jj + 1) * W],
                                           start=False, stop=(kt == NKT - 1))
                        S.inc(ii, "pe", ("p2", "pv", i, gg))
                    S.wait(pe, ("p2", "linvbf", i))
                    S.inc(pe.matmul(out=ps_q[:, 0:W], lhsT=ones1_bf[:], rhs=lrow_bf[:],
                                    start=True, stop=True), "pe", ("p2", "linvb", i))
                    S.wait(pe, ("p2", "osb", i))
                    for og_o in range(OG):
                        if og_o >= 4:
                            S.wait(pe, ("p2", "x1", i, og_o - 4))
                        S.inc(pe.matmul(out=wo_slots[og_o % 4],
                                        lhsT=wo_b[:, og_o * 128:(og_o + 1) * 128],
                                        rhs=O_sb[:], start=True, stop=True),
                              "pe", ("p2", "wo", i, og_o))
                    S.wait(pe, ("p2", "cast", i))
                    for og in range(OG):
                        pe.matmul(out=ps_l[0:1, :], lhsT=ones_bf[:],
                                  rhs=xb_(i, og), start=(og == 0),
                                  stop=(og == OG - 1), tile_position=(0, 0))
                        i1 = pe.matmul(out=ps_l[32:33, :], lhsT=ones_bf[:],
                                       rhs=xsq[:, og * W:(og + 1) * W],
                                       start=(og == 0), stop=(og == OG - 1),
                                       tile_position=(0, 32))
                    S.inc(i1, "pe", ("p2", "stats", i))
                    S.wait(pe, ("p2", "lnvec", i))
                    pe.matmul(out=ps_x[:, W:2 * W], lhsT=ones1_bf[:], rhs=stat_mn[:],
                              start=True, stop=True)
                    S.inc(pe.matmul(out=ps_x[:, 2 * W:3 * W], lhsT=ones1_bf[:],
                                    rhs=stat_isb[:], start=True, stop=True),
                          "pe", ("p2", "bc", i))
                    S.wait(pe, ("p2", "xqb", i))
                    for og_o in range(OG):
                        if og_o >= 4:
                            S.wait(pe, ("p2", "fevac", i, og_o - 4))
                        for og in range(OG):
                            ii = pe.matmul(
                                out=ffn_slots[og_o % 4],
                                lhsT=kqb[:, og * D + og_o * 128:og * D + (og_o + 1) * 128],
                                rhs=xqb[:, og * W:(og + 1) * W],
                                start=(og == 0), stop=(og == OG - 1))
                        S.inc(ii, "pe", ("p2", "ffn", i, og_o))

            def dveA(dve):
                if not _H("p0cast"):
                    return
                dve.memset(ones_bf[:], 1.0)
                dve.memset(ones1_bf[:], 1.0)
                dve.memset(smalls[0:1, 5:6], 0.0)
                dve.memset(smalls[0:1, 6:7], 0.0)
                dve.memset(smalls[0:1, 10:11], 0.0)
                dve.memset(smalls[0:1, 11:12], 0.0)
                S.wait(dve, ("p0", "small"))
                dve.reciprocal(out=rs_in_b[:], in_=sin_sb[:])
                dve.reciprocal(out=rs_k_b[:], in_=sk_sb[:])
                dve.tensor_mul(out=sprod_b[:], in0=sin_sb[:], in1=sk_sb[:])
                for src, dst in ((tmk_fm, omk_fm), (tmv_fm, omv_fm), (tmr_fm, omr_fm)):
                    dve.tensor_scalar(out=dst[:], in0=src[:], scalar1=-1.0,
                                      scalar2=1.0, op0=ALU.mult, op1=ALU.add)
                S.inc(dve.engine_nop(), "dve", ("p0", "scal"))
                S.wait(dve, ("p0", "wa"))
                dve.tensor_copy(out=wq_b[:], in_=xcf[:, 0:D])
                dve.tensor_copy(out=wk_b[:], in_=xcf[:, D:2 * D])
                dve.tensor_copy(out=wv_b[:], in_=xcf[:, 2 * D:3 * D])
                S.inc(dve.tensor_copy(out=wo_b[:], in_=xcf[:, 3 * D:4 * D]),
                      "dve", ("p0", "wcast"))
                if not _H("p0sw"):
                    return
                S.wait(dve, ("p0", "sw"))
                for wi, swv in enumerate((SWq, SWk, SWv)):
                    dve.tensor_copy(out=swv[:], in_=ps_l[0:1, wi * 128:(wi + 1) * 128])
                S.inc(dve.engine_nop(), "dve", ("p0", "bce"))
                if not _H("p0kq"):
                    return
                for hh in range(4):
                    S.wait(dve, ("p0", "kq8_h", hh))
                    S.inc(dve.tensor_copy(out=kqb[:, hh * 2 * D:(hh + 1) * 2 * D],
                                          in_=xq8[:]),
                          "dve", ("p0", "kqb_h", hh))
                S.inc(dve.engine_nop(), "dve", ("p0", "kqb"))
                if not _H("p1stats"):
                    return
                for i in range(NCH):
                    S.wait(dve, ("p1", "x", i))
                    if i >= 1 and _H("p1proj"):
                        S.wait(dve, ("p1", "proj", i - 1))
                    elif i >= 1:
                        S.wait(dve, ("p1", "stats", i - 1))
                    if i == 0:
                        S.wait(dve, ("p0", "kqb"))
                    S.se(dve.tensor_copy(out=xb_(i), in_=xf(i)), dve, "vsf")
                    S.inc(dve.tensor_mul(out=xsq[:], in0=xb_(i), in1=xb_(i)),
                          "dve", ("p1", "cast", i))
                    S.wait(dve, ("p1", "stats", i))
                    dve.tensor_scalar_mul(out=stat_mu[:], in0=ps_l[0:1, :],
                                          scalar1=1.0 / D)
                    S.se(dve.tensor_scalar_mul(out=stat_m2[:], in0=ps_l[32:33, :],
                                               scalar1=1.0 / D), dve, "vsf")
                    S.se(dve.tensor_mul(out=stat_is[:], in0=stat_mu[:],
                                        in1=stat_mu[:]), dve, "vsf")
                    S.inc(dve.tensor_sub(out=lrow[:], in0=stat_m2[:], in1=stat_is[:]),
                          "dve", ("p1", "var", i))
                    if not _H("p1proj"):
                        continue
                    S.inc(dve.tensor_scalar_mul(out=stat_bv[:], in0=stat_mu[:],
                                                scalar1=-1.0),
                          "dve", ("p1", "bvec", i))
                    S.wait(dve, ("p1", "proj", i))
                    if i >= 1 and _H("p1rope"):
                        S.wait(dve, ("p1", "rope", i - 1))
                    S.se(dve.tensor_copy(out=av_sb[:], in_=ps_x[:, 0:W]), dve, "vsf")
                    dve.tensor_mul(out=qstg[:], in0=ps_q[:, 0:W], in1=av_sb[:])
                    dve.tensor_mul(out=kstg[:], in0=ps_q[:, W:2 * W], in1=av_sb[:])
                    S.inc(dve.tensor_mul(out=vstg[:], in0=ps_q[:, 2 * W:3 * W],
                                         in1=av_sb[:]),
                          "dve", ("p1", "evac", i))
                    if not _H("p1rope"):
                        continue
                    S.wait(dve, ("p1", "swmm", i))
                    dve.tensor_copy(out=qsw[:], in_=ps_x[:, 2 * W:3 * W])
                    S.inc(dve.tensor_copy(out=ksw[:], in_=ps_o[:]),
                          "dve", ("p1", "swe", i))
                    S.sew(dve, "dve")
                    if not _H("p1tr"):
                        continue
                    S.wait(dve, ("p1", "tr", i, KPC - 1))
                    for j in range(KPC):
                        S.inc(dve.tensor_copy(
                            out=v_tok[:, (i * KPC + j) * 128:(i * KPC + j + 1) * 128],
                            in_=tr_slots[j]),
                            "dve", ("p1", "trevac", i, j))
                    if not _H("p1nrm"):
                        continue
                    qa, qb = (5, 10) if i % 2 == 0 else (10, 5)
                    ka, kb = (6, 11) if i % 2 == 0 else (11, 6)
                    S.inc(dve.tensor_mul(out=sq_scr[:], in0=qstg[:], in1=qstg[:]),
                          "dve", ("p1", "sq", i))
                    S.wait(dve, ("p1", "nq", i))
                    S.se(dve.reduce_max(out=smalls[0:1, 9:10], in_=ps_o[0:1, :],
                                        axis=AX.X), dve, "vsf")
                    dve.tensor_max(out=smalls[0:1, qb:qb + 1],
                                   in0=smalls[0:1, qa:qa + 1],
                                   in1=smalls[0:1, 9:10])
                    S.inc(dve.tensor_mul(out=sq_scr[:], in0=kstg[:], in1=kstg[:]),
                          "dve", ("p1", "sqk", i))
                    S.wait(dve, ("p1", "nk", i))
                    S.se(dve.reduce_max(out=smalls[0:1, 9:10], in_=ps_o[32:33, :],
                                        axis=AX.X), dve, "vsf")
                    S.inc(dve.tensor_max(out=smalls[0:1, kb:kb + 1],
                                         in0=smalls[0:1, ka:ka + 1],
                                         in1=smalls[0:1, 9:10]),
                          "dve", ("p1", "nrm", i))
                if not _H("p2"):
                    return
                S.se(dve.engine_nop(), dve, "vsf")
                S.inc(dve.tensor_mul(out=smalls[0:1, 7:8], in0=smalls[0:1, 10:11],
                                     in1=smalls[0:1, 11:12]),
                      "dve", ("p1", "mprod"))
                S.wait(dve, ("p1", "negc_act"))
                nsrc = bass.AP(smalls, 13, [[16, 1], [0, W]])
                S.inc(dve.tensor_copy(out=lrow_bf[0:1, :], in_=nsrc),
                      "dve", ("p1", "negc_v"))
                S.wait(dve, ("p1", "negc_mm"))
                S.inc(dve.tensor_copy(out=negC_b[:], in_=ps_o[:, 0:1]),
                      "dve", ("p1", "done"))
                if upto < 2:
                    return
                for i in range(NCH):
                    S.wait(dve, ("p2", "x", i))
                    S.wait(dve, ("p2", "pv", i, NSG - 1))
                    S.inc(dve.tensor_copy(out=lrow[:], in_=ps_l[0:1, :]),
                          "dve", ("p2", "l", i))
                    S.wait(dve, ("p2", "linv_act", i))
                    S.inc(dve.tensor_copy(out=lrow_bf[:], in_=lrow[:]),
                          "dve", ("p2", "linvbf", i))
                    S.wait(dve, ("p2", "linvb", i))
                    S.se(dve.tensor_copy(out=linv128[:], in_=ps_q[:, 0:W]),
                         dve, "vsf")
                    S.inc(dve.tensor_mul(out=O_sb[:], in0=ps_o[:], in1=linv128[:]),
                          "dve", ("p2", "osb", i))
                    for og_o in range(OG):
                        S.wait(dve, ("p2", "wo", i, og_o))
                        S.inc(dve.tensor_add(out=xg[:, og_o * W:(og_o + 1) * W],
                                             in0=wo_slots[og_o % 4],
                                             in1=xf(i, og_o)),
                              "dve", ("p2", "x1", i, og_o))
                    S.se(dve.engine_nop(), dve, "vsf")
                    S.se(dve.tensor_copy(out=xb_(i), in_=xg[:]), dve, "vsf")
                    S.inc(dve.tensor_mul(out=xsq[:], in0=xb_(i), in1=xb_(i)),
                          "dve", ("p2", "cast", i))
                    S.wait(dve, ("p2", "stats", i))
                    dve.tensor_scalar_mul(out=stat_mu[:], in0=ps_l[0:1, :],
                                          scalar1=1.0 / D)
                    S.se(dve.tensor_scalar_mul(out=stat_m2[:], in0=ps_l[32:33, :],
                                               scalar1=1.0 / D), dve, "vsf")
                    S.se(dve.tensor_mul(out=stat_is[:], in0=stat_mu[:],
                                        in1=stat_mu[:]), dve, "vsf")
                    S.inc(dve.tensor_sub(out=lrow[:], in0=stat_m2[:], in1=stat_is[:]),
                          "dve", ("p2", "var", i))
                    S.wait(dve, ("p2", "bc", i))
                    S.se(dve.tensor_copy(out=bcast_sb[:], in_=ps_x[:, W:3 * W]),
                         dve, "vsf")
                    mu_ap = bass.AP(bcast_sb, 0, [[2 * W, 128], [0, 8], [1, W]])
                    is_ap = bass.AP(bcast_sb, W, [[2 * W, 128], [0, 8], [1, W]])
                    S.se(dve.tensor_sub(out=xsq[:], in0=xg[:], in1=mu_ap),
                         dve, "vsf")
                    S.inc(dve.tensor_mul(out=xhb[:], in0=xsq[:], in1=is_ap),
                          "dve", ("p2", "xhb", i))
                    for og_o in range(OG):
                        S.wait(dve, ("p2", "fevac", i, og_o))
                        yb = ybf[:, (og_o % 2) * W:(og_o % 2) * W + W]
                        S.inc(dve.tensor_add(out=xf(i, og_o),
                                             in0=xg[:, og_o * W:(og_o + 1) * W],
                                             in1=yb),
                              "dve", ("p2", "z", i, og_o))
                    S.inc(dve.engine_nop(), "dve", ("p2", "resid", i))

            def actA(act):
                if not _H("p0vec"):
                    return
                S.wait(act, ("p0", "scal"))
                S.se(act.activation(out=vsm[:, 0:OG], in_=lam_fm[:], func=ACT.Exp),
                     act, "asf")
                S.se(act.activation(out=lam_fm[:], in_=vsm[:, 0:OG], func=ACT.Exp,
                                    scale=-1.0), act, "asf")
                S.se(act.activation(out=vsm[:, OG:2 * OG], in_=eu_fm[:],
                                    func=ACT.Exp), act, "asf")
                S.se(act.copy(out=eu_fm[:], in_=vsm[:, OG:2 * OG]), act, "asf")
                act.copy(out=lam_bf[:], in_=lam_fm[:])
                S.inc(act.copy(out=eu_bf[:], in_=eu_fm[:]), "act", ("p0", "vecs"))
                if not _H("p0kq"):
                    return
                S.wait(act, ("p0", "fker"))
                S.wait(act, ("p0", "bce"))
                for hh in range(4):
                    if hh > 0:
                        S.wait(act, ("p0", "kqb_h", hh - 1))
                    S.inc(act.activation(out=xq8[:],
                                         in_=kqb[:, hh * 2 * D:(hh + 1) * 2 * D],
                                         func=ACT.Copy, scale=rs_k_b[:, 0:1]),
                          "act", ("p0", "kq8_h", hh))
                if not _H("p1stats"):
                    return
                for i in range(NCH):
                    S.wait(act, ("p1", "var", i))
                    S.se(act.activation(out=stat_m2[:], in_=lrow[:], func=ACT.Ln,
                                        bias=acst[0:1, 0:1]), act, "asf")
                    S.inc(act.activation(out=stat_isb[:], in_=stat_m2[:],
                                         func=ACT.Exp, scale=acst[0:1, 1:2]),
                          "act", ("p1", "isb", i))
                if not _H("p2"):
                    return
                S.wait(act, ("p1", "mprod"))
                S.se(act.activation(out=smalls[0:1, 8:9], in_=smalls[0:1, 7:8],
                                    func=ACT.Ln), act, "asf")
                S.se(act.activation(out=smalls[0:1, 12:13], in_=smalls[0:1, 8:9],
                                    func=ACT.Exp, scale=acst[0:1, 3:4],
                                    bias=acst[0:1, 4:5]), act, "asf")
                S.inc(act.mul(out=smalls[0:1, 13:14], in_=smalls[0:1, 12:13],
                              mul=-1.0),
                      "act", ("p1", "negc_act"))
                if upto < 2:
                    return
                for i in range(NCH):
                    for g in range(NSG):
                        ks = [g * SG + j for j in range(SG) if g * SG + j < NKT]
                        S.wait(act, ("p2", "s", i, g))
                        sps = ps_q if g % 2 == 0 else ps_x
                        S.inc(act.activation(
                            out=ptil[:, (g % 2) * SG * W:(g % 2) * SG * W + len(ks) * W],
                            in_=sps[:, 0:len(ks) * W], func=ACT.Exp,
                            scale=acst[:, 2:3], bias=negC_b[:, 0:1]),
                            "act", ("p2", "exp", i, g))
                    S.wait(act, ("p2", "l", i))
                    S.se(act.activation(out=srow[:], in_=lrow[:], func=ACT.Ln),
                         act, "asf")
                    S.inc(act.activation(out=lrow[:], in_=srow[:], func=ACT.Exp,
                                         scale=-1.0),
                          "act", ("p2", "linv_act", i))
                    S.wait(act, ("p2", "var", i))
                    S.se(act.activation(out=stat_m2[:], in_=lrow[:], func=ACT.Ln,
                                        bias=acst[0:1, 0:1]), act, "asf")
                    S.se(act.activation(out=stat_isb[:], in_=stat_m2[:],
                                        func=ACT.Exp, scale=acst[0:1, 1:2]),
                         act, "asf")
                    S.inc(act.activation(out=stat_mn[:], in_=ps_l[0:1, :],
                                         func=ACT.Copy, scale=acst[0:1, 7:8]),
                          "act", ("p2", "lnvec", i))
                    S.wait(act, ("p2", "xhb", i))
                    S.se(act.activation(out=xq8[:], in_=xhb[:, 0:4 * W],
                                        func=ACT.Copy, scale=rs_in_b[:, 0:1]),
                         act, "asf")
                    S.inc(act.activation(out=xqb[:, 0:4 * W], in_=xq8[:],
                                         func=ACT.Copy),
                          "act", ("p2", "xqba", i))
                    S.sew(act, "act")
                    S.se(act.activation(out=xq8[:], in_=xhb[:, 4 * W:8 * W],
                                        func=ACT.Copy, scale=rs_in_b[:, 0:1]),
                         act, "asf")
                    S.inc(act.activation(out=xqb[:, 4 * W:8 * W], in_=xq8[:],
                                         func=ACT.Copy),
                          "act", ("p2", "xqb", i))
                    for og_o in range(OG):
                        S.wait(act, ("p2", "ffn", i, og_o))
                        if og_o >= 2:
                            S.wait(act, ("p2", "z", i, og_o - 2))
                        S.inc(act.activation(
                            out=ybf[:, (og_o % 2) * W:(og_o % 2) * W + W],
                            in_=ffn_slots[og_o % 4], func=ACT.Identity,
                            scale=sprod_b[:, 0:1],
                            bias=bias_fm[:, og_o:og_o + 1]),
                            "act", ("p2", "fevac", i, og_o))

            def gpA(gp):
                if not _H("p1rope"):
                    return
                for i in range(NCH):
                    S.wait(gp, ("p1", "evac", i))
                    S.wait(gp, ("p1", "swe", i))
                    S.wait(gp, ("p1", "cs", i))
                    cw = cos_b[:, (i % 2) * W:(i % 2) * W + W]
                    swt = sin_b[:, (i % 2) * W:(i % 2) * W + W]
                    for ti_, (stg, ssw, dst) in enumerate(
                            ((qstg, qsw, q_rot), (kstg, ksw, k_rot))):
                        lo = dst[0:64, i * W:(i + 1) * W]
                        hi = dst[64:128, i * W:(i + 1) * W]
                        gp.tensor_mul(out=u2_scr[0:64, 0:W], in0=stg[0:64, :],
                                      in1=cw[0:64, :])
                        gp.tensor_mul(out=u2_scr[0:64, W:2 * W], in0=ssw[0:64, :],
                                      in1=swt[0:64, :])
                        gp.tensor_mul(out=u2_scr[64:128, 0:W], in0=ssw[64:128, :],
                                      in1=cw[64:128, :])
                        S.se(gp.tensor_mul(out=u2_scr[64:128, W:2 * W],
                                           in0=stg[64:128, :],
                                           in1=swt[64:128, :]), gp, "gsf")
                        gp.tensor_sub(out=lo, in0=u2_scr[0:64, 0:W],
                                      in1=u2_scr[0:64, W:2 * W])
                        hi_add = gp.tensor_add(out=hi, in0=u2_scr[64:128, 0:W],
                                               in1=u2_scr[64:128, W:2 * W])
                        if ti_ == 0:
                            S.se(hi_add, gp, "gsf")
                        else:
                            S.inc(hi_add, "gp", ("p1", "rope", i))
                            S.sew(gp, "gp")

            S.mode = "count"
            for nm, fn in (("sp", spA), ("pe", peA), ("dve", dveA), ("act", actA),
                           ("gp", gpA)):
                S._cur = nm
                fn(_DummyEng(nm))
            S._cur = None
            snapA = dict(S.counts)
            S.counts = {}
            S.mode = "emit"
            blockA = st0.enter_context(nc.Block("prog"))
            blockA.sync(spA)
            blockA.tensor(peA)
            blockA.vector(dveA)
            blockA.scalar(actA)
            blockA.gpsimd(gpA)
            assert S.counts == snapA, "part A emit/count mismatch"

        if upto >= 3:
            with ExitStack() as stB:
                def sbB(name, shape, dt):
                    return stB.enter_context(nc.sbuf_tensor(name, list(shape), dt))

                def psB(name, shape, dt=F32):
                    return stB.enter_context(nc.psum_tensor(name, list(shape), dt))

                Wkb3 = sbB("Wkb3", (128, OG * D), BF16)
                Wvb3 = sbB("Wvb3", (128, OG * D), BF16)
                Wrb3 = sbB("Wrb3", (128, OG * D), BF16)
                Wob3 = sbB("Wob3", (128, OG * D), BF16)
                x3cf = sbB("x3cf", (128, 2 * OG * W), F32)
                xcb3 = sbB("xcb3", (128, OG * W), BF16)
                xsq3 = sbB("xsq3", (128, OG * W), BF16)
                zh3 = sbB("zh3", (128, OG * W), BF16)
                zsh3 = sbB("zsh3", (128, OG * W), BF16)
                mix3 = sbB("mix3", (128, OG * W), BF16)
                b3 = sbB("b3", (128, OG * W), F32)
                a3 = sbB("a3", (128, OG * W), BF16)
                Aog = sbB("Aog", (128, W + 8), F32)
                Bog = sbB("Bog", (128, W + 8), F32)
                Eog = sbB("Eog", (128, 2 * (W + 8)), F32)
                den_s = sbB("den_s", (128, 2 * W), F32)
                scr3 = sbB("scr3", (128, 2 * W), F32)
                xq84 = sbB("xq84", (128, 2 * W), FP8)
                ybf3 = sbB("ybf3", (128, W), BF16)
                bcast3 = sbB("bcast3", (128, 2 * W), BF16)
                carryA = sbB("carryA", (128, OG), BF16)
                carryB = sbB("carryB", (128, OG), BF16)
                carryE = sbB("carryE", (128, OG), F32)
                zcarry = sbB("zcarry", (128, OG), BF16)

                pa = psB("pa", (128, 4 * W))
                ps_st3 = psB("ps_st3", (128, W))
                ps_bc3 = psB("ps_bc3", (128, 2 * W))
                pa_s = [pa[:, 0:W], pa[:, W:2 * W], pa[:, 2 * W:3 * W],
                        pa[:, 3 * W:4 * W]]

                def x3(i, og=None):
                    b = (i % 2) * OG * W
                    if og is None:
                        return x3cf[:, b:b + OG * W]
                    return x3cf[:, b + og * W: b + (og + 1) * W]

                def ogs(t, og):
                    return t[:, og * W:(og + 1) * W]

                def lam_ap(og, bf=False):
                    src = lam_bf if bf else lam_fm
                    return bass.AP(src, og, [[OG, 128], [0, W]])

                P3END = {"p3ln3": ("zh",), "p3mix": ("a", OG - 1),
                         "p3scan": ("ry",), "p3wo": ("x3", OG - 1),
                         "p3ln4": ("xqb4",)}

                def _p3end(i):
                    st_ = _HALT[0]
                    tail = P3END[st_]
                    return ("p3", tail[0], i) + tuple(tail[1:])

                def spB(sp):
                    if not _H("p3w"):
                        return
                    for wi, wname in enumerate(("WkT", "WvT", "WrT", "WoT")):
                        for h in range(2):
                            if wi == 0:
                                S.wait(sp, ("p2", "x2out", NCH - 1))
                            else:
                                S.wait(sp, ("p3", "wc", wi - 1, h))
                            dst = bass.AP(x3cf, h * 4 * D,
                                          [[2 * OG * W, 128], [D, 4], [1, D]])
                            src = bass.AP(dram[wname], h * 4 * 128 * D,
                                          [[D, 128], [128 * D, 4], [1, D]])
                            S.inc(sp.dma_start(out=dst, in_=src),
                                  "dwl" if h == 0 else "dq",
                                  ("p3", "wl", wi, h), 16)
                    if not _H("p3ln3"):
                        return
                    for i in range(NCH + 2):
                        if not _H("full"):
                            if i >= 2 and i < NCH:
                                S.wait(sp, _p3end(i - 2))
                            if i < NCH:
                                pass
                            else:
                                continue
                        if _H("full") and i >= 2:
                            j = i - 2
                            S.wait(sp, ("p3", "x4", j))
                            src = bass.AP(x3cf, (j % 2) * OG * W,
                                          [[2 * OG * W, 128], [W, OG], [1, W]])
                            dst = bass.AP(dram["y"], j * W,
                                          [[T, 128], [128 * T, OG], [1, W]])
                            S.inc(sp.dma_start(out=dst, in_=src), "dout",
                                  ("p3", "yout", j), 16)
                        if i < NCH:
                            if i == 0:
                                S.wait(sp, ("p3", "wc", 3, 1))
                            if i >= 2 and _H("full"):
                                S.wait(sp, ("p3", "yout", i - 2))
                            dst = bass.AP(x3cf, (i % 2) * OG * W,
                                          [[2 * OG * W, 128], [W, OG], [1, W]])
                            src = bass.AP(dram["x2d"], i * W,
                                          [[T, 128], [128 * T, OG], [1, W]])
                            S.inc(sp.dma_start(out=dst, in_=src),
                                  "dxa" if i % 2 == 0 else "dxb",
                                  ("p3", "x", i), 16)
                    if _H("full"):
                        S.wait(sp, ("p3", "yout", NCH - 1))

                def peB(pe):
                    if not _H("p3ln3"):
                        return
                    for i in range(NCH):
                        S.wait(pe, ("p3", "cast3", i))
                        if i >= 1:
                            S.wait_opt(pe, ("p3", "var4", i - 1))
                            if not _H("full"):
                                S.wait(pe, _p3end(i - 1))
                        for og in range(OG):
                            pe.matmul(out=ps_st3[0:1, :], lhsT=ones_bf[:],
                                      rhs=ogs(xcb3, og), start=(og == 0),
                                      stop=(og == OG - 1), tile_position=(0, 0))
                            i1 = pe.matmul(out=ps_st3[32:33, :], lhsT=ones_bf[:],
                                           rhs=ogs(xsq3, og), start=(og == 0),
                                           stop=(og == OG - 1),
                                           tile_position=(0, 32))
                        S.inc(i1, "pe", ("p3", "stats3", i))
                        S.wait(pe, ("p3", "lnvec3", i))
                        pe.matmul(out=ps_bc3[:, 0:W], lhsT=ones1_bf[:],
                                  rhs=stat_mn[:], start=True, stop=True)
                        S.inc(pe.matmul(out=ps_bc3[:, W:2 * W], lhsT=ones1_bf[:],
                                        rhs=stat_isb[:], start=True, stop=True),
                              "pe", ("p3", "bc3", i))
                        def stage_emit(stage, wmat, mixkey):
                            if mixkey is not None:
                                S.wait(pe, mixkey)
                            rhs_t = {"k2": mix3, "v2": mix3, "r2": mix3,
                                     "wo": a3, "ffn2": zh3}[stage]
                            for co in range(OG):
                                gk = _slot_guard(i, stage, co)
                                if gk is not None:
                                    S.wait_opt(pe, gk)
                                for og in range(OG):
                                    if stage == "wo" and co == 0:
                                        S.wait(pe, ("p3", "yp", i, og))
                                    ii = pe.matmul(
                                        out=pa_s[co % 4],
                                        lhsT=wmat[:, og * D + co * 128:
                                                  og * D + (co + 1) * 128],
                                        rhs=ogs(rhs_t, og),
                                        start=(og == 0), stop=(og == OG - 1))
                                S.inc(ii, "pe", ("p3", stage, i, co))

                        if not _H("p3mix"):
                            continue
                        stage_emit("k2", Wkb3, ("p3", "mixk", i))
                        stage_emit("v2", Wvb3, ("p3", "mixv", i))
                        stage_emit("r2", Wrb3, ("p3", "mixr", i))
                        if not _H("p3wo"):
                            continue
                        stage_emit("wo", Wob3, None)
                        if not _H("p3ln4"):
                            continue
                        S.wait(pe, ("p3", "cast4", i))
                        for og in range(OG):
                            pe.matmul(out=ps_st3[0:1, :], lhsT=ones_bf[:],
                                      rhs=ogs(mix3, og), start=(og == 0),
                                      stop=(og == OG - 1), tile_position=(0, 0))
                            i1 = pe.matmul(out=ps_st3[32:33, :], lhsT=ones_bf[:],
                                           rhs=ogs(xsq3, og), start=(og == 0),
                                           stop=(og == OG - 1),
                                           tile_position=(0, 32))
                        S.inc(i1, "pe", ("p3", "stats4", i))
                        S.wait(pe, ("p3", "lnvec4", i))
                        pe.matmul(out=ps_bc3[:, 0:W], lhsT=ones1_bf[:],
                                  rhs=stat_mn[:], start=True, stop=True)
                        S.inc(pe.matmul(out=ps_bc3[:, W:2 * W], lhsT=ones1_bf[:],
                                        rhs=stat_isb[:], start=True, stop=True),
                              "pe", ("p3", "bc4", i))
                        if not _H("full"):
                            continue
                        stage_emit("ffn2", kqb, ("p3", "xqb4", i))

                def _slot_guard(i, stage, co):
                    order = ["k2", "v2", "r2", "wo", "ffn2"]
                    cons = {"k2": ("act", "b"), "v2": ("dve", "a"),
                            "r2": ("act", "e1"), "wo": ("dve", "x3"),
                            "ffn2": ("act", "fevac4")}
                    si = order.index(stage)
                    if co >= 4:
                        pst, pco, pi = stage, co - 4, i
                    elif si > 0:
                        pst, pco, pi = order[si - 1], co + 4, i
                    else:
                        if i == 0:
                            return None
                        pst, pco, pi = "ffn2", co + 4, i - 1
                    _, ckey = cons[pst]
                    return ("p3", ckey, pi, pco)

                def dveB(dve):
                    if not _H("p3w"):
                        return
                    for wi, wsb in enumerate((Wkb3, Wvb3, Wrb3, Wob3)):
                        for h in range(2):
                            S.wait(dve, ("p3", "wl", wi, h))
                            S.inc(dve.tensor_copy(
                                out=wsb[:, h * 4 * D:(h + 1) * 4 * D],
                                in_=x3cf[:, h * 4 * D:(h + 1) * 4 * D]),
                                "dve", ("p3", "wc", wi, h))
                    dve.memset(carryA[:], 0.0)
                    dve.memset(carryB[:], 0.0)
                    dve.memset(carryE[:], 0.0)
                    dve.memset(zcarry[:], 0.0)
                    if not _H("p3ln3"):
                        return
                    for i in range(NCH):
                        S.wait(dve, ("p3", "x", i))
                        if False:
                            pass
                        S.se(dve.tensor_copy(out=xcb3[:], in_=x3(i)), dve, "vsf")
                        S.inc(dve.tensor_mul(out=xsq3[:], in0=xcb3[:], in1=xcb3[:]),
                              "dve", ("p3", "cast3", i))
                        S.wait(dve, ("p3", "stats3", i))
                        dve.tensor_scalar_mul(out=stat_mu[:], in0=ps_st3[0:1, :],
                                              scalar1=1.0 / D)
                        S.se(dve.tensor_scalar_mul(out=stat_m2[:],
                                                   in0=ps_st3[32:33, :],
                                                   scalar1=1.0 / D), dve, "vsf")
                        S.se(dve.tensor_mul(out=stat_is[:], in0=stat_mu[:],
                                            in1=stat_mu[:]), dve, "vsf")
                        S.inc(dve.tensor_sub(out=lrow[:], in0=stat_m2[:],
                                             in1=stat_is[:]),
                              "dve", ("p3", "var3", i))
                        S.wait(dve, ("p3", "bc3", i))
                        S.se(dve.tensor_copy(out=bcast3[:], in_=ps_bc3[:]),
                             dve, "vsf")
                        mu_ap = bass.AP(bcast3, 0, [[2 * W, 128], [0, 8], [1, W]])
                        is_ap = bass.AP(bcast3, W, [[2 * W, 128], [0, 8], [1, W]])
                        S.se(dve.tensor_sub(out=xsq3[:], in0=x3(i), in1=mu_ap),
                             dve, "vsf")
                        if i >= 1:
                            if _H("full"):
                                S.wait(dve, ("p3", "ffn2", i - 1, OG - 1))
                            elif _H("p3ln4"):
                                S.wait(dve, ("p3", "xqb4", i - 1))
                        S.inc(dve.tensor_mul(out=zh3[:], in0=xsq3[:], in1=is_ap),
                              "dve", ("p3", "zh", i))
                        S.sew(dve, "dve")
                        if not _H("p3mix"):
                            continue
                        dstz = bass.AP(zsh3, 1, [[OG * W, 128], [W, OG], [1, W - 1]])
                        srcz = bass.AP(zh3, 0, [[OG * W, 128], [W, OG], [1, W - 1]])
                        dve.tensor_copy(out=dstz, in_=srcz)
                        dstc = bass.AP(zsh3, 0, [[OG * W, 128], [W, OG]])
                        S.se(dve.tensor_copy(out=dstc, in_=zcarry[:]), dve, "vsf")
                        srcc = bass.AP(zh3, W - 1, [[OG * W, 128], [W, OG]])
                        S.se(dve.tensor_copy(out=zcarry[:], in_=srcc), dve, "vsf")
                        S.inc(dve.engine_nop(), "dve", ("p3", "zsh", i))
                        def mix_emit(mkey, tm, om, guard):
                            if guard is not None:
                                S.wait(dve, guard)
                            for og in range(OG):
                                ii = dve.tensor_scalar_mul(out=ogs(xcb3, og),
                                                           in0=ogs(zh3, og),
                                                           scalar1=tm[:, og:og + 1])
                            S.se(ii, dve, "vsf")
                            for og in range(OG):
                                ii = dve.scalar_tensor_tensor(
                                    out=ogs(mix3, og), in0=ogs(zsh3, og),
                                    scalar=om[:, og:og + 1], in1=ogs(xcb3, og),
                                    op0=ALU.mult, op1=ALU.add)
                            S.se(ii, dve, "vsf")
                            S.inc(dve.engine_nop(), "dve", ("p3", mkey, i))

                        def a_emit(co):
                            S.wait(dve, ("p3", "b", i, co))
                            S.wait(dve, ("p3", "v2", i, co))
                            S.inc(dve.tensor_mul(out=ogs(a3, co), in0=pa_s[co % 4],
                                                 in1=ogs(b3, co)),
                                  "dve", ("p3", "a", i, co))

                        mix_emit("mixk", tmk_fm, omk_fm,
                                 (("p3", "stats4", i - 1) if _H("p3ln4")
                                  else ("p3", "r2", i - 1, OG - 1))
                                 if (i >= 1 and _H("p3mix")) else None)
                        mix_emit("mixv", tmv_fm, omv_fm, ("p3", "k2", i, OG - 1))
                        for co in range(4):
                            a_emit(co)
                        mix_emit("mixr", tmr_fm, omr_fm, ("p3", "v2", i, OG - 1))
                        for co in range(4, OG):
                            a_emit(co)
                        if not _H("p3scan"):
                            continue
                        def y_phase(oy):
                            eby = (oy % 2) * (W + 8)
                            sc = scr3[:, (oy % 2) * W:(oy % 2) * W + W]
                            S.wait(dve, ("p3", "rden", i, oy))
                            S.se(dve.tensor_mul(out=sc, in0=Eog[:, eby:eby + W],
                                                in1=ogs(xcb3, oy)),
                                 dve, "vsf")
                            S.inc(dve.tensor_mul(out=ogs(a3, oy), in0=ogs(b3, oy),
                                                 in1=sc),
                                  "dve", ("p3", "yp", i, oy))
                            S.sew(dve, "dve")

                        for og in range(OG):
                            if og >= 2:
                                y_phase(og - 2)
                            ds = den_s[:, (og % 2) * W:(og % 2) * W + W]
                            eb = (og % 2) * (W + 8)
                            dve.tensor_copy(out=Bog[:, 0:1], in_=carryB[:, og:og + 1])
                            dve.tensor_copy(out=Eog[:, eb:eb + 1],
                                            in_=carryE[:, og:og + 1])
                            dve.tensor_copy(out=Aog[:, 0:1], in_=carryA[:, og:og + 1])
                            dve.tensor_tensor_scan(
                                out=Bog[:, 1:W + 1], data0=lam_ap(og),
                                data1=ogs(b3, og), initial=carryB[:, og:og + 1],
                                op0=ALU.mult, op1=ALU.add)
                            dve.tensor_tensor_scan(
                                out=Eog[:, eb + 1:eb + W + 1],
                                data0=lam_ap(og, bf=True),
                                data1=ogs(b3, og), initial=carryE[:, og:og + 1],
                                op0=ALU.mult, op1=ALU.max)
                            S.se(dve.tensor_tensor_scan(
                                out=Aog[:, 1:W + 1], data0=lam_ap(og),
                                data1=ogs(a3, og), initial=carryA[:, og:og + 1],
                                op0=ALU.mult, op1=ALU.add), dve, "vsf")
                            dve.tensor_copy(out=carryB[:, og:og + 1],
                                            in_=Bog[:, W:W + 1])
                            dve.tensor_copy(out=carryE[:, og:og + 1],
                                            in_=Eog[:, eb + W:eb + W + 1])
                            dve.tensor_copy(out=carryA[:, og:og + 1],
                                            in_=Aog[:, W:W + 1])
                            dve.scalar_tensor_tensor(
                                out=ds, in0=Bog[:, 0:W],
                                scalar=eu_fm[:, og:og + 1], in1=ogs(b3, og),
                                op0=ALU.mult, op1=ALU.add)
                            S.se(dve.scalar_tensor_tensor(
                                out=ogs(zsh3, og), in0=Eog[:, eb:eb + W],
                                scalar=eu_bf[:, og:og + 1], in1=ogs(b3, og),
                                op0=ALU.mult, op1=ALU.max), dve, "vsf")
                            dve.scalar_tensor_tensor(
                                out=ogs(b3, og), in0=Aog[:, 0:W],
                                scalar=eu_fm[:, og:og + 1], in1=ogs(a3, og),
                                op0=ALU.mult, op1=ALU.add)
                            S.inc(dve.scalar_tensor_tensor(
                                out=Eog[:, eb:eb + W],
                                in0=ogs(zsh3, og), scalar=1e-6, in1=ds,
                                op0=ALU.mult, op1=ALU.add),
                                "dve", ("p3", "den", i, og))
                            S.sew(dve, "dve")
                        for oy in (OG - 2, OG - 1):
                            y_phase(oy)
                        S.inc(dve.engine_nop(), "dve", ("p3", "ry", i))
                        if not _H("p3wo"):
                            continue
                        for co in range(OG):
                            S.wait(dve, ("p3", "wo", i, co))
                            S.inc(dve.tensor_add(out=ogs(b3, co), in0=pa_s[co % 4],
                                                 in1=x3(i, co)),
                                  "dve", ("p3", "x3", i, co))
                        if not _H("p3ln4"):
                            continue
                        S.se(dve.engine_nop(), dve, "vsf")
                        S.se(dve.tensor_copy(out=mix3[:], in_=b3[:]), dve, "vsf")
                        S.inc(dve.tensor_mul(out=xsq3[:], in0=mix3[:], in1=mix3[:]),
                              "dve", ("p3", "cast4", i))
                        S.wait(dve, ("p3", "stats4", i))
                        dve.tensor_scalar_mul(out=stat_mu[:], in0=ps_st3[0:1, :],
                                              scalar1=1.0 / D)
                        S.se(dve.tensor_scalar_mul(out=stat_m2[:],
                                                   in0=ps_st3[32:33, :],
                                                   scalar1=1.0 / D), dve, "vsf")
                        S.se(dve.tensor_mul(out=stat_is[:], in0=stat_mu[:],
                                            in1=stat_mu[:]), dve, "vsf")
                        S.inc(dve.tensor_sub(out=lrow[:], in0=stat_m2[:],
                                             in1=stat_is[:]),
                              "dve", ("p3", "var4", i))
                        S.wait(dve, ("p3", "bc4", i))
                        S.se(dve.tensor_copy(out=bcast3[:], in_=ps_bc3[:]),
                             dve, "vsf")
                        S.se(dve.tensor_sub(out=xsq3[:], in0=b3[:], in1=mu_ap),
                             dve, "vsf")
                        S.inc(dve.tensor_mul(out=zh3[:], in0=xsq3[:], in1=is_ap),
                              "dve", ("p3", "xh4", i))

                def actB(act):
                    if not _H("p3ln3"):
                        return
                    for i in range(NCH):
                        S.wait(act, ("p3", "var3", i))
                        S.se(act.activation(out=stat_m2[:], in_=lrow[:], func=ACT.Ln,
                                            bias=acst[0:1, 0:1]), act, "asf")
                        S.se(act.activation(out=stat_isb[:], in_=stat_m2[:],
                                            func=ACT.Exp, scale=acst[0:1, 1:2]),
                             act, "asf")
                        S.inc(act.activation(out=stat_mn[:], in_=ps_st3[0:1, :],
                                             func=ACT.Copy, scale=acst[0:1, 7:8]),
                              "act", ("p3", "lnvec3", i))
                        if not _H("p3mix"):
                            continue
                        for co in range(OG):
                            S.wait(act, ("p3", "k2", i, co))
                            if i >= 1 and _H("full"):
                                S.wait(act, ("p3", "z4", i - 1, co))
                            S.inc(act.activation(out=ogs(b3, co), in_=pa_s[co % 4],
                                                 func=ACT.Exp),
                                  "act", ("p3", "b", i, co))
                        for co in range(OG):
                            S.wait(act, ("p3", "r2", i, co))
                            ii = act.activation(out=scr3[:, (co % 2) * W:
                                                         (co % 2) * W + W],
                                                in_=pa_s[co % 4], func=ACT.Exp,
                                                scale=-1.0)
                            S.inc(ii, "act", ("p3", "e1", i, co))
                            S.sew(act, "act")
                            s0 = scr3[:, (co % 2) * W:(co % 2) * W + W]
                            s1 = scr3[:, ((co + 1) % 2) * W:((co + 1) % 2) * W + W]
                            S.se(act.activation(out=s1, in_=s0, func=ACT.Ln,
                                                bias=acst[:, 5:6]), act, "asf")
                            ii2 = act.activation(out=ogs(xcb3, co), in_=s1,
                                                 func=ACT.Exp, scale=-1.0)
                            if co < OG - 1:
                                S.se(ii2, act, "asf")
                        S.inc(ii2, "act", ("p3", "r", i))
                        if not _H("p3scan"):
                            continue
                        for og in range(OG):
                            S.wait(act, ("p3", "den", i, og))
                            ds = den_s[:, (og % 2) * W:(og % 2) * W + W]
                            eb = (og % 2) * (W + 8)
                            S.se(act.activation(out=ds, in_=Eog[:, eb:eb + W],
                                                func=ACT.Ln),
                                 act, "asf")
                            S.inc(act.activation(out=Eog[:, eb:eb + W], in_=ds,
                                                 func=ACT.Exp, scale=-1.0),
                                  "act", ("p3", "rden", i, og))
                        if not _H("p3ln4"):
                            continue
                        S.wait(act, ("p3", "var4", i))
                        S.se(act.activation(out=stat_m2[:], in_=lrow[:], func=ACT.Ln,
                                            bias=acst[0:1, 0:1]), act, "asf")
                        S.se(act.activation(out=stat_isb[:], in_=stat_m2[:],
                                            func=ACT.Exp, scale=acst[0:1, 1:2]),
                             act, "asf")
                        S.inc(act.activation(out=stat_mn[:], in_=ps_st3[0:1, :],
                                             func=ACT.Copy, scale=acst[0:1, 7:8]),
                              "act", ("p3", "lnvec4", i))
                        S.wait(act, ("p3", "xh4", i))
                        for hh in range(4):
                            if hh > 0:
                                S.sew(act, "act")
                            S.se(act.activation(
                                out=xq84[:],
                                in_=zh3[:, hh * 2 * W:(hh + 1) * 2 * W],
                                func=ACT.Copy, scale=rs_in_b[:, 0:1]),
                                act, "asf")
                            S.inc(act.activation(
                                out=zh3[:, hh * 2 * W:(hh + 1) * 2 * W],
                                in_=xq84[:], func=ACT.Copy),
                                "act", ("p3", "xqb4", i) if hh == 3 else
                                ("p3", "xqb4h", i, hh))
                        if not _H("full"):
                            continue
                        for og_o in range(OG):
                            S.wait(act, ("p3", "ffn2", i, og_o))
                            if og_o >= 1:
                                S.wait(act, ("p3", "z4", i, og_o - 1))
                            S.inc(act.activation(
                                out=ybf3[:],
                                in_=pa_s[og_o % 4], func=ACT.Identity,
                                scale=sprod_b[:, 0:1],
                                bias=bias_fm[:, og_o:og_o + 1]),
                                "act", ("p3", "fevac4", i, og_o))

                def gpB(gp):
                    if not _H("full"):
                        return
                    for i in range(NCH):
                        for og in range(OG):
                            S.wait(gp, ("p3", "fevac4", i, og))
                            S.wait(gp, ("p3", "x3", i, og))
                            S.inc(gp.tensor_add(out=x3(i, og), in0=ogs(b3, og),
                                                in1=ybf3[:]),
                                  "gp", ("p3", "z4", i, og))
                            S.sew(gp, "gp")
                        S.inc(gp.engine_nop(), "gp", ("p3", "x4", i))

                S.mode = "count"
                for nm, fn in (("sp", spB), ("pe", peB), ("dve", dveB),
                               ("act", actB), ("gp", gpB)):
                    S._cur = nm
                    fn(_DummyEng(nm))
                S._cur = None
                snapB = dict(S.counts)
                S.counts = dict(snapA)
                S.mode = "emit"
                blockA.sync(spB)
                blockA.tensor(peB)
                blockA.vector(dveB)
                blockA.scalar(actB)
                blockA.gpsimd(gpB)
                assert S.counts == snapB, "part B emit/count mismatch"

    return nc


_ROPE_PERM = np.concatenate([np.arange(0, HD, 2), np.arange(1, HD, 2)])


def _host_prep(inputs):
    bf = ml_dtypes.bfloat16
    x = np.asarray(inputs["L0_attn_a_x"])
    wq = np.asarray(inputs["L0_attn_a_wq"])[_ROPE_PERM]
    wk = np.asarray(inputs["L0_attn_a_wk"])[_ROPE_PERM]
    wv = np.asarray(inputs["L0_attn_a_wv"])
    wo = np.asarray(inputs["L0_attn_a_wo"])

    def fm(v, dtype=np.float32):
        v = np.asarray(v).reshape(-1)
        assert v.size == D
        return np.ascontiguousarray(v.reshape(OG, 128).T.astype(dtype, copy=False))

    common = {
        "wqT": np.ascontiguousarray(wq.T),
        "wkT": np.ascontiguousarray(wk.T),
        "wvT": np.ascontiguousarray(wv.T),
        "woT": np.ascontiguousarray(wo.T),
        "WkT": np.ascontiguousarray(np.asarray(inputs["L0_attn_b_Wk"]).T),
        "WvT": np.ascontiguousarray(np.asarray(inputs["L0_attn_b_Wv"]).T),
        "WrT": np.ascontiguousarray(np.asarray(inputs["L0_attn_b_Wr"]).T),
        "WoT": np.ascontiguousarray(np.asarray(inputs["L0_attn_b_Wo"]).T),
        "fker": np.ascontiguousarray(np.asarray(inputs["L0_ffn_kernel"])),
        "td": fm(inputs["L0_attn_b_time_decay"]),
        "tf": fm(inputs["L0_attn_b_time_first"]),
        "tmk": fm(inputs["L0_attn_b_time_mix_k"]),
        "tmv": fm(inputs["L0_attn_b_time_mix_v"]),
        "tmr": fm(inputs["L0_attn_b_time_mix_r"]),
        "fbias": fm(np.asarray(inputs["L0_ffn_bias"]).astype(np.float32)),
        "s_in": np.ascontiguousarray(np.broadcast_to(
            np.asarray(inputs["L0_ffn_input_scale"], np.float32).reshape(1, 1),
            (128, 1))),
        "s_k": np.ascontiguousarray(np.broadcast_to(
            np.asarray(inputs["L0_ffn_kernel_scale"], np.float32).reshape(1, 1),
            (128, 1))),
        "ident": np.eye(128, dtype=np.float32),
        "swp": np.ascontiguousarray(np.roll(np.eye(128), 64, axis=0).astype(bf)),
        "aconsts": np.ascontiguousarray(np.broadcast_to(np.array(
            [1e-5, -0.5, ISQ, 0.5, -np.log(128.0) / 2.0, 1.0, 0.0, 1.0 / 1024.0],
            np.float32), (128, 8))),
    }
    j = np.arange(64, dtype=np.float64)
    invf = 10000.0 ** (-2.0 * j / 128.0)
    t = np.arange(T, dtype=np.float64)
    ang = np.where(t[None, :] >= 1, (t[None, :] - 1) * invf[:, None], 0.0)
    cosT = np.cos(ang)
    sinT = np.sin(ang)
    cosT[:, 0] = 1.0
    sinT[:, 0] = 0.0
    common["cosT"] = np.ascontiguousarray(
        np.concatenate([cosT, sinT], axis=0).astype(bf))
    common["sinT"] = np.ascontiguousarray(
        np.concatenate([sinT, cosT], axis=0).astype(bf))

    in_maps = []
    for b in range(8):
        m = dict(common)
        m["xT"] = np.ascontiguousarray(x[b].T)
        in_maps.append(m)
    return in_maps


def kernel(**inputs):
    in_maps = _host_prep(inputs)
    nc = build_nc(upto=UPTO, taps=TAPS)
    res = run_bass_kernel_spmd(nc, in_maps, core_ids=list(range(8)))
    out = np.empty((8, T, D), dtype=np.float32)
    for b in range(8):
        out[b] = np.asarray(res.results[b]["y"]).T
    return out
